# version 01 (final)
# baseline (speedup 1.0000x reference)
import os
import numpy as np
import ml_dtypes
import concourse.bass as bass
import concourse.mybir as mybir
from concourse.bass_utils import run_bass_kernel_spmd
from contextlib import ExitStack

F32 = mybir.dt.float32
BF16 = mybir.dt.bfloat16
I32 = mybir.dt.int32
AF = mybir.ActivationFunctionType
ALU = mybir.AluOpType

D = 4096
HF = 11008
NJT = HF // 128
NIC = D // 128
TOK = 1024
HALF = TOK // 2
NCH = TOK // 128
NH = 32
KVH = 8
NCORE = 8
RMS_EPS = 1e-5
LN_EPS = 1e-5

DBG = bool(int(os.environ.get("MK_DEBUG", "0")))


def _tile_qw(qw):
    t = qw.reshape(NIC, 128, NJT, 128).transpose(2, 1, 0, 3)
    return np.ascontiguousarray(t.reshape(NJT * 128, NIC * 128))


def _srow(s):
    t = s.reshape(NIC, NJT, 128).transpose(1, 0, 2)
    return np.ascontiguousarray(t.reshape(NJT, NIC * 128))


def _tile_wT(w, n_oc):
    wT = np.ascontiguousarray(w.T)
    t = wT.reshape(NIC, 128, n_oc, 128).transpose(2, 1, 0, 3)
    return np.ascontiguousarray(t.reshape(n_oc * 128, NIC * 128))


def make_in_maps(I):
    hs = np.asarray(I["L0_attn_hidden_states"])
    qwg_t = _tile_qw(np.asarray(I["L0_ffn_qw_gate"]))
    qwu_t = _tile_qw(np.asarray(I["L0_ffn_qw_up"]))
    qwd = np.ascontiguousarray(np.asarray(I["L0_ffn_qw_down"]))
    srg = _srow(np.asarray(I["L0_ffn_s_gate"]))
    sru = _srow(np.asarray(I["L0_ffn_s_up"]))
    srd = np.ascontiguousarray(np.asarray(I["L0_ffn_s_down"]))
    zg = np.ascontiguousarray(np.asarray(I["L0_ffn_z_gate"]))
    zu = np.ascontiguousarray(np.asarray(I["L0_ffn_z_up"]))
    zd = np.ascontiguousarray(np.asarray(I["L0_ffn_z_down"]))
    sgn = np.ascontiguousarray(np.asarray(I["L0_ffn_s_gate"]))
    sun = np.ascontiguousarray(np.asarray(I["L0_ffn_s_up"]))
    wq_t = _tile_wT(np.asarray(I["L0_attn_Wq"]), NH)
    wk_t = _tile_wT(np.asarray(I["L0_attn_Wk"]), KVH)
    wv_t = _tile_wT(np.asarray(I["L0_attn_Wv"]), KVH)
    wg_t = _tile_wT(np.asarray(I["L0_attn_Wg"]), NH)
    wo_t = _tile_wT(np.asarray(I["L0_attn_Wo"]), NH)
    winT = np.ascontiguousarray(np.asarray(I["L0_attn_Win"]).T)
    dtb = np.ascontiguousarray(np.asarray(I["L0_attn_dt_bias"]).reshape(1, 32))
    alog = np.ascontiguousarray(np.asarray(I["L0_attn_A_log_bias"]).reshape(1, 32))
    gnorm = np.ascontiguousarray(np.asarray(I["L0_attn_gnorm_w"]).reshape(128, 1))

    inv = 1.0 / (10000.0 ** (np.arange(0, 128, 2, dtype=np.float64) / 128.0))

    in_maps = []
    for ci in range(NCORE):
        b, t0 = ci // 4, (ci % 4) * TOK
        x0 = np.ascontiguousarray(hs[b, t0:t0 + TOK, :].T.astype(np.float32))
        pos = np.arange(t0, t0 + TOK, dtype=np.float64)
        ang = pos[:, None] * inv[None, :]
        cos = np.concatenate([np.cos(ang), np.cos(ang)], axis=1).T
        sin = np.concatenate([np.sin(ang), np.sin(ang)], axis=1).T
        mask = np.zeros((32, 8), dtype=np.float32)
        for j in range(8):
            if (j // 4 == ci // 4) and (j < ci):
                mask[:, j] = 1.0
        in_maps.append(dict(
            x0=x0,
            qwg=qwg_t, qwu=qwu_t, qwd=qwd, srg=srg, sru=sru, srd=srd,
            zg=zg, zu=zu, zd=zd, sgn=sgn, sun=sun,
            wq=wq_t, wk=wk_t, wv=wv_t, wg=wg_t, wo=wo_t, winT=winT,
            dtb=dtb, alog=alog, gnorm=gnorm,
            cosf=np.ascontiguousarray(cos.astype(np.float32)),
            sinf=np.ascontiguousarray(sin.astype(np.float32)),
            cmask=mask,
        ))
    return in_maps


class Ctx:
    pass


def build_program():
    nc = bass.Bass("TRN2", target_bir_lowering=False,
                   detect_race_conditions=not bool(int(os.environ.get("MK_NORACE", "0"))))
    c = Ctx()
    c.nc = nc

    c.x0_d = nc.dram_tensor("x0", [D, TOK], F32, kind="ExternalInput").ap()
    c.out_d = nc.dram_tensor("out_fm", [D, TOK], F32, kind="ExternalOutput").ap()
    c.qwg_d = nc.dram_tensor("qwg", [NJT * 128, NIC * 128], I32, kind="ExternalInput").ap()
    c.qwu_d = nc.dram_tensor("qwu", [NJT * 128, NIC * 128], I32, kind="ExternalInput").ap()
    c.qwd_d = nc.dram_tensor("qwd", [HF, D], I32, kind="ExternalInput").ap()
    c.srg_d = nc.dram_tensor("srg", [NJT, NIC * 128], F32, kind="ExternalInput").ap()
    c.sru_d = nc.dram_tensor("sru", [NJT, NIC * 128], F32, kind="ExternalInput").ap()
    c.srd_d = nc.dram_tensor("srd", [NJT, D], F32, kind="ExternalInput").ap()
    c.zg_d = nc.dram_tensor("zg", [32, HF], I32, kind="ExternalInput").ap()
    c.zu_d = nc.dram_tensor("zu", [32, HF], I32, kind="ExternalInput").ap()
    c.zd_d = nc.dram_tensor("zd", [NJT, D], I32, kind="ExternalInput").ap()
    c.sgn_d = nc.dram_tensor("sgn", [32, HF], F32, kind="ExternalInput").ap()
    c.sun_d = nc.dram_tensor("sun", [32, HF], F32, kind="ExternalInput").ap()
    c.wq_d = nc.dram_tensor("wq", [NH * 128, NIC * 128], F32, kind="ExternalInput").ap()
    c.wk_d = nc.dram_tensor("wk", [KVH * 128, NIC * 128], F32, kind="ExternalInput").ap()
    c.wv_d = nc.dram_tensor("wv", [KVH * 128, NIC * 128], F32, kind="ExternalInput").ap()
    c.wg_d = nc.dram_tensor("wg", [NH * 128, NIC * 128], F32, kind="ExternalInput").ap()
    c.wo_d = nc.dram_tensor("wo", [NH * 128, NIC * 128], F32, kind="ExternalInput").ap()
    c.winT_d = nc.dram_tensor("winT", [D, 32], F32, kind="ExternalInput").ap()
    c.dtb_d = nc.dram_tensor("dtb", [1, 32], F32, kind="ExternalInput").ap()
    c.alog_d = nc.dram_tensor("alog", [1, 32], F32, kind="ExternalInput").ap()
    c.gnorm_d = nc.dram_tensor("gnorm", [128, 1], F32, kind="ExternalInput").ap()
    c.cos_d = nc.dram_tensor("cosf", [128, TOK], F32, kind="ExternalInput").ap()
    c.sin_d = nc.dram_tensor("sinf", [128, TOK], F32, kind="ExternalInput").ap()
    c.cmask_d = nc.dram_tensor("cmask", [32, 8], F32, kind="ExternalInput").ap()

    c.x1_d = nc.dram_tensor("x1_hbm", [D, TOK], F32, kind="Internal").ap()
    c.x2_d = nc.dram_tensor("x2_hbm", [D, TOK], F32, kind="Internal").ap()
    c.h_d = nc.dram_tensor("h_hbm", [2 * NJT * 128, HALF], BF16, kind="Internal").ap()
    c.w1d_d = nc.dram_tensor("w1d_hbm", [NJT * 128, D], BF16, kind="Internal").ap()
    c.q_d = nc.dram_tensor("q_hbm", [NCH * NH * 128, 128], BF16, kind="Internal").ap()
    c.gate_d = nc.dram_tensor("gate_hbm", [NH * 128, TOK], BF16, kind="Internal").ap()
    c.cst_d = nc.dram_tensor("cst_hbm", [NCH * 128, NH * 128], BF16, kind="Internal").ap()
    c.gt_d = nc.dram_tensor("gt_hbm", [NCH * 32, 128], F32, kind="Internal").ap()
    c.row_d = nc.dram_tensor("row_hbm", [1, 512], F32, kind="Internal").ap()
    c.w8_d = nc.dram_tensor("w8_hbm", [1, 256], F32, kind="Internal").ap()
    c.ccin_d = nc.dram_tensor("ccin", [NH * 128 + 1, 128], F32, kind="Internal")
    c.ccout_d = nc.dram_tensor("ccout", [NCORE * (NH * 128 + 1), 128], F32, kind="Internal")

    e32 = np.zeros((NIC, 128, NIC), dtype=ml_dtypes.bfloat16)
    for i in range(NIC):
        e32[i, :, i] = 1.0
    c.e32_d = nc.inline_tensor(e32.reshape(NIC * 128, NIC), "e32").ap()
    e86 = np.zeros((NJT, 128, NJT), dtype=ml_dtypes.bfloat16)
    for i in range(NJT):
        e86[i, :, i] = 1.0
    c.e86_d = nc.inline_tensor(e86.reshape(NJT * 128, NJT), "e86").ap()
    jj, ii = np.meshgrid(np.arange(128), np.arange(128), indexing="ij")
    triu = (jj <= ii)
    c.triu_d = nc.inline_tensor(triu.astype(ml_dtypes.bfloat16), "triu").ap()
    c.mask_d = nc.inline_tensor(np.where(triu, 0.0, -1e30).astype(np.float32), "trimask").ap()
    c.ident_d = nc.inline_tensor(np.eye(128, dtype=ml_dtypes.bfloat16), "ident").ap()
    c.identf_d = nc.inline_tensor(np.eye(128, dtype=np.float32), "identf").ap()

    c.dbg = {}
    if DBG:
        for nm, shape, dt in [
            ("xnb1", [128, NIC * TOK], BF16), ("x1", [D, TOK], F32),
            ("x2", [D, TOK], F32), ("kfm", [128, KVH * TOK], BF16),
            ("gall", [128, NCH * 32], F32), ("dtall", [128, NCH * 32], F32),
            ("S", [128, NH * 128], F32), ("ygfm", [128, NH * TOK], BF16),
            ("cdall", [128, NCH * 32], F32), ("Z", [128, NH * 128], F32),
            ("Sfin", [128, NH * 128], F32), ("cst", [NCH * 128, NH * 128], BF16),
            ("wb", [128, 256], F32),
        ]:
            c.dbg[nm] = nc.dram_tensor("dbg_" + nm, shape, dt, kind="ExternalOutput").ap()

    es = ExitStack()
    E = es.enter_context

    c.xnb = E(nc.sbuf_tensor("xnb", [128, NIC * TOK], BF16))
    c.mb = E(nc.sbuf_tensor("mb", [128, TOK], F32))
    c.rb = E(nc.sbuf_tensor("rb", [128, TOK], F32))
    c.xs = E(nc.sbuf_tensor("xs", [32, TOK], BF16))
    c.ones_b = E(nc.sbuf_tensor("ones_b", [1, 128], BF16))
    c.ones_f = E(nc.sbuf_tensor("ones_f", [1, 128], F32))
    c.onec_b = E(nc.sbuf_tensor("onec_b", [128, 1], BF16))
    c.triu_s = E(nc.sbuf_tensor("triu_s", [128, 128], BF16))
    c.mask_s = E(nc.sbuf_tensor("mask_s", [128, 128], F32))
    c.ident_s = E(nc.sbuf_tensor("ident_s", [128, 128], BF16))
    c.identf_s = E(nc.sbuf_tensor("identf_s", [128, 128], F32))
    c.gnorm_s = E(nc.sbuf_tensor("gnorm_s", [128, 1], F32))
    c.dtb_b = E(nc.sbuf_tensor("dtb_b", [128, 32], F32))
    c.A_b = E(nc.sbuf_tensor("A_b", [128, 32], F32))
    c.row_s = E(nc.sbuf_tensor("row_s", [1, 512], F32))
    c.cvals = E(nc.sbuf_tensor("cvals", [128, 8], F32))
    c.cval_list = [LN_EPS, -1.0, 1.0, 1.0 / 128.0, -0.5, 0.5]
    for i, val in enumerate(c.cval_list):
        nc.const_aps.register(F32, val, c.cvals[:, i:i + 1])

    c._pool = [E(nc.semaphore(f"sp{i}")) for i in range(20)]
    c._pmap = {}

    def sem(name):
        if name not in c._pmap:
            c._pmap[name] = c._pool[len(c._pmap)]
            assert len(c._pmap) <= len(c._pool), "semaphore pool exhausted"
        return c._pmap[name]
    c.sem = sem

    c._gsems = {}

    def gsem(name):
        if name not in c._gsems:
            c._gsems[name] = E(nc.semaphore("g_" + name))
        return c._gsems[name]
    c.gsem = gsem

    c.sbar = E(nc.semaphore("sbar"))
    c._nbar = [0]
    c.block = E(nc.Block())

    def _bar():
        c._nbar[0] += 1
        tgt = 5 * c._nbar[0]
        for sec in (c.block.sync, c.block.vector, c.block.scalar,
                    c.block.tensor, c.block.gpsimd):
            @sec
            def _(e):
                e.drain()
                e.sem_inc(c.sbar, 1)
                e.wait_ge(c.sbar, tgt)

    def boundary():
        _bar()
        used = list(c._pmap.values())
        if used:
            @c.block.sync
            def _(sy):
                for h in used:
                    sy.sem_clear(h)
            _bar()
        c._pmap = {}
    c.boundary = boundary

    phases = os.environ.get("MK_PHASES", "all")
    on = (lambda nm: True) if phases == "all" else (lambda nm: nm in phases.split(","))
    emit_p0(c)
    c.boundary()
    if on("l1"):
        emit_ln_stats(c, "l1", c.x0_d)
    if on("a1"):
        emit_ln_apply(c, "a1", c.x0_d)
    if on("xA"):
        emit_xs(c, "xA")
    if on("f1"):
        emit_ffn(c, "f1", x_in=c.x0_d, x_out=c.x1_d,
                 dbg_x=c.dbg.get("x1"), dbg_xnb=c.dbg.get("xnb1"))
    if on("l2"):
        emit_ln_stats(c, "l2", c.x1_d)
    if on("a2"):
        emit_ln_apply(c, "a2", c.x1_d)
    if on("mx"):
        emit_mixer(c)
    if on("l3"):
        emit_ln_stats(c, "l3", c.x2_d)
    if on("a3"):
        emit_ln_apply(c, "a3", c.x2_d)
    if on("xB"):
        emit_xs(c, "xB")
    if on("f2"):
        emit_ffn(c, "f2", x_in=c.x2_d, x_out=c.out_d)

    es.close()
    return nc, c


def emit_p0(c):
    nc, block, sem = c.nc, c.block, c.sem
    ld, cv, mm = sem("p0_ld"), sem("p0_cv"), sem("p0_mm")

    with nc.psum_tensor("p0_ps", [128, 64], F32) as ps:
        @block.sync
        def _(sy):
            sy.dma_start(out=c.triu_s[:, :], in_=c.triu_d).then_inc(ld, 16)
            sy.dma_start(out=c.mask_s[:, :], in_=c.mask_d).then_inc(ld, 16)
            sy.dma_start(out=c.ident_s[:, :], in_=c.ident_d).then_inc(ld, 16)
            sy.dma_start(out=c.identf_s[:, :], in_=c.identf_d).then_inc(ld, 16)
            sy.dma_start(out=c.gnorm_s[:, :], in_=c.gnorm_d).then_inc(ld, 16)
            sy.dma_start(out=c.row_s[:, 0:32], in_=c.dtb_d).then_inc(ld, 16)
            sy.dma_start(out=c.row_s[:, 32:64], in_=c.alog_d).then_inc(ld, 16)

        @block.vector
        def _(v):
            v.memset(c.ones_b[:, :], 1.0)
            v.memset(c.onec_b[:, :], 1.0)
            for i, val in enumerate(c.cval_list):
                v.memset(c.cvals[:, i:i + 1], val)
            v.memset(c.ones_f[:, :], 1.0).then_inc(cv, 1)

        @block.tensor
        def _(t):
            t.wait_ge(ld, 16 * 7)
            t.wait_ge(cv, 1)
            t.matmul(ps[:, :], c.ones_f[:, :], c.row_s[:, 0:64],
                     start=True, stop=True).then_inc(mm, 1)

        @block.vector
        def _(v):
            v.wait_ge(mm, 1)
            v.tensor_copy(out=c.dtb_b[:, :], in_=ps[:, 0:32]).then_inc(cv, 1)

        @block.scalar
        def _(s):
            s.wait_ge(mm, 1)
            s.activation(out=c.A_b[:, :], in_=ps[:, 32:64], func=AF.Exp).then_inc(cv, 1)

        @block.vector
        def _(v):
            v.wait_ge(cv, 3)
            v.tensor_scalar_mul(out=c.A_b[:, :], in0=c.A_b[:, :], scalar1=-1.0).then_inc(cv, 1)


def emit_ln_stats(c, tag, x_d):
    nc, block, sem = c.nc, c.block, c.sem
    ld, cv, mm, fin = sem(tag + "_ld"), sem(tag + "_cv"), sem(tag + "_mm"), sem(tag + "_fin")
    with ExitStack() as ar:
        E = ar.enter_context
        xch = [E(nc.sbuf_tensor(f"{tag}_x{b}", [128, TOK], F32)) for b in range(2)]
        xb = E(nc.sbuf_tensor(tag + "_xb", [128, TOK], BF16))
        xq = E(nc.sbuf_tensor(tag + "_xq", [128, TOK], BF16))
        srow = E(nc.sbuf_tensor(tag + "_srow", [1, TOK], F32))
        qrow = E(nc.sbuf_tensor(tag + "_qrow", [1, TOK], F32))
        m_bf = E(nc.sbuf_tensor(tag + "_mbf", [1, TOK], BF16))
        r_bf = E(nc.sbuf_tensor(tag + "_rbf", [1, TOK], BF16))
        p_s = E(nc.psum_tensor(tag + "_ps", [1, TOK], F32))
        p_q = E(nc.psum_tensor(tag + "_pq", [1, TOK], F32))
        p_b = E(nc.psum_tensor(tag + "_pb", [128, TOK], F32))
        p_b2 = E(nc.psum_tensor(tag + "_pb2", [128, TOK], F32))

        ldp = [sem(tag + "_ldp0"), sem(tag + "_ldp1")]

        @block.sync
        def _(sy):
            for ch in range(NIC):
                if ch >= 2:
                    sy.wait_ge(cv, 2 * (ch - 1))
                sy.dma_start(out=xch[ch % 2][:, :],
                             in_=x_d[ch * 128:(ch + 1) * 128, :]).then_inc(ldp[ch % 2], 16)

        @block.vector
        def _(v):
            for ch in range(NIC):
                v.wait_ge(ldp[ch % 2], 16 * (ch // 2 + 1))
                if ch >= 1:
                    v.wait_ge(mm, 4 * ch)
                v.tensor_copy(out=xb[:, :], in_=xch[ch % 2][:, :])
                v.tensor_tensor(out=xq[:, :], in0=xb[:, :], in1=xb[:, :],
                                op=ALU.mult).then_inc(cv, 2)

        @block.tensor
        def _(t):
            for ch in range(NIC):
                t.wait_ge(cv, 2 * (ch + 1))
                for hf in range(2):
                    t.matmul(p_s[:, hf * HALF:(hf + 1) * HALF], c.onec_b[:, :],
                             xb[:, hf * HALF:(hf + 1) * HALF],
                             start=(ch == 0), stop=(ch == NIC - 1))
                ins = None
                for hf in range(2):
                    ins = t.matmul(p_q[:, hf * HALF:(hf + 1) * HALF], c.onec_b[:, :],
                                   xq[:, hf * HALF:(hf + 1) * HALF],
                                   start=(ch == 0), stop=(ch == NIC - 1))
                ins.then_inc(mm, 4)

        @block.vector
        def _(v):
            v.wait_ge(mm, 4 * NIC)
            v.tensor_scalar_mul(out=srow[:, :], in0=p_s[:, :], scalar1=1.0 / D)
            v.tensor_tensor(out=qrow[:, :], in0=srow[:, :], in1=srow[:, :], op=ALU.mult)
            v.scalar_tensor_tensor(out=qrow[:, :], in0=p_q[:, :], scalar=1.0 / D,
                                   in1=qrow[:, :], op0=ALU.mult,
                                   op1=ALU.subtract).then_inc(cv, 1)

        @block.scalar
        def _(s):
            s.wait_ge(cv, 2 * NIC + 1)
            s.activation(out=qrow[:, :], in_=qrow[:, :], func=AF.Sqrt,
                         bias=LN_EPS).then_inc(fin, 1)

        @block.vector
        def _(v):
            v.wait_ge(fin, 1)
            v.reciprocal(out=qrow[:, :], in_=qrow[:, :])
            v.tensor_copy(out=m_bf[:, :], in_=srow[:, :])
            v.tensor_copy(out=r_bf[:, :], in_=qrow[:, :]).then_inc(cv, 1)

        @block.tensor
        def _(t):
            t.wait_ge(cv, 2 * NIC + 2)
            for hf in range(2):
                t.matmul(p_b[:, hf * HALF:(hf + 1) * HALF], c.ones_b[:, :],
                         m_bf[:, hf * HALF:(hf + 1) * HALF], start=True, stop=True)
            ins = None
            for hf in range(2):
                ins = t.matmul(p_b2[:, hf * HALF:(hf + 1) * HALF], c.ones_b[:, :],
                               r_bf[:, hf * HALF:(hf + 1) * HALF], start=True, stop=True)
            ins.then_inc(mm, 1)

        @block.vector
        def _(v):
            v.wait_ge(mm, 4 * NIC + 1)
            v.tensor_copy(out=c.mb[:, :], in_=p_b[:, :])
            v.tensor_copy(out=c.rb[:, :], in_=p_b2[:, :]).then_inc(cv, 1)

    c.boundary()


def emit_ln_apply(c, tag, x_d):
    nc, block, sem = c.nc, c.block, c.sem
    ld, cv = sem(tag + "_ld"), sem(tag + "_cv")
    with ExitStack() as ar:
        E = ar.enter_context
        xch = [E(nc.sbuf_tensor(f"{tag}_x{b}", [128, TOK], F32)) for b in range(2)]
        tmp = E(nc.sbuf_tensor(tag + "_t", [128, TOK], F32))

        ldp = [sem(tag + "_ldp0"), sem(tag + "_ldp1")]

        @block.sync
        def _(sy):
            for ch in range(NIC):
                if ch >= 2:
                    sy.wait_ge(cv, 2 * (ch - 1))
                sy.dma_start(out=xch[ch % 2][:, :],
                             in_=x_d[ch * 128:(ch + 1) * 128, :]).then_inc(ldp[ch % 2], 16)

        @block.vector
        def _(v):
            for ch in range(NIC):
                v.wait_ge(ldp[ch % 2], 16 * (ch // 2 + 1))
                v.tensor_tensor(out=tmp[:, :], in0=xch[ch % 2][:, :], in1=c.mb[:, :],
                                op=ALU.subtract)
                v.tensor_tensor(out=c.xnb[:, ch * TOK:(ch + 1) * TOK], in0=tmp[:, :],
                                in1=c.rb[:, :], op=ALU.mult).then_inc(cv, 2)

    c.boundary()


def emit_xs(c, tag):
    nc, block, sem = c.nc, c.block, c.sem
    ld, mm, cv = sem(tag + "_ld"), sem(tag + "_mm"), sem(tag + "_cv")
    with ExitStack() as ar:
        E = ar.enter_context
        e32_s = E(nc.sbuf_tensor(tag + "_e32", [128, NIC * NIC], BF16))
        p_xs = E(nc.psum_tensor(tag + "_pxs", [32, TOK], F32))

        @block.sync
        def _(sy):
            sy.dma_start(out=e32_s.ap().rearrange("p (a m) -> p a m", a=NIC),
                         in_=c.e32_d.rearrange("(a p) m -> p a m", p=128)).then_inc(ld, 16)

        @block.tensor
        def _(t):
            t.wait_ge(ld, 16)
            ins = None
            for hf in range(2):
                for ic in range(NIC):
                    ins = t.matmul(p_xs[0:32, hf * HALF:(hf + 1) * HALF],
                                   e32_s[:, ic * NIC:(ic + 1) * NIC],
                                   c.xnb[:, ic * TOK + hf * HALF: ic * TOK + (hf + 1) * HALF],
                                   start=(ic == 0), stop=(ic == NIC - 1))
            ins.then_inc(mm, 1)

        @block.vector
        def _(v):
            v.wait_ge(mm, 1)
            v.tensor_copy(out=c.xs[:, :], in_=p_xs[0:32, :]).then_inc(cv, 1)

    c.boundary()


def emit_ffn(c, tag, x_in, x_out, dbg_x=None, dbg_xnb=None):
    nc, block, sem = c.nc, c.block, c.sem

    with ExitStack() as ar:
        E = ar.enter_context
        zsg = E(nc.sbuf_tensor(tag + "_zsg", [32, HF], BF16))
        zsu = E(nc.sbuf_tensor(tag + "_zsu", [32, HF], BF16))
        inner = ExitStack()
        ztmp = inner.enter_context(nc.sbuf_tensor(tag + "_ztmp", [32, HF], I32))
        stmp = inner.enter_context(nc.sbuf_tensor(tag + "_stmp", [32, HF], F32))
        inner.close()
        qwg_s = E(nc.sbuf_tensor(tag + "_qwg", [128, NIC * 128], BF16))
        qwu_s = E(nc.sbuf_tensor(tag + "_qwu", [128, NIC * 128], BF16))
        srg_s = E(nc.sbuf_tensor(tag + "_srg", [1, NIC * 128], BF16))
        sru_s = E(nc.sbuf_tensor(tag + "_sru", [1, NIC * 128], BF16))
        w1g = [E(nc.sbuf_tensor(f"{tag}_w1g{b}", [128, NIC * 128], BF16)) for b in range(2)]
        w1u = [E(nc.sbuf_tensor(f"{tag}_w1u{b}", [128, NIC * 128], BF16)) for b in range(2)]
        sg_sb = [E(nc.sbuf_tensor(f"{tag}_sg{b}", [128, HALF], F32)) for b in range(2)]
        h_sb = [E(nc.sbuf_tensor(f"{tag}_h{b}", [128, TOK], BF16)) for b in range(2)]
        sfb_ps = [E(nc.psum_tensor(f"{tag}_sfb{b}", [128, 1024], F32)) for b in range(2)]
        p_gate = E(nc.psum_tensor(tag + "_pg", [128, TOK], F32))
        p_up = E(nc.psum_tensor(tag + "_pu", [128, TOK], F32))

        s_zl = gsem_ = c.gsem(tag + "_zl")
        s_zs = sem(tag + "_zs")
        s_ldq = c.gsem(tag + "_ldq")
        s_ldr = c.gsem(tag + "_ldr")
        s_bc = sem(tag + "_bc")
        s_dqg = sem(tag + "_dqg")
        s_dqu = sem(tag + "_dqu")
        s_mm = sem(tag + "_mm")
        s_silu = sem(tag + "_silu")
        s_h = sem(tag + "_h")
        s_hstp = [sem(tag + "_hst0"), sem(tag + "_hst1")]
        s_dlq = c.gsem(tag + "_dlq")
        s_dls = c.gsem(tag + "_dls")
        s_dbc = sem(tag + "_dbc")
        s_ddq = sem(tag + "_ddq")
        s_dstp = [sem(tag + "_dst0"), sem(tag + "_dst1")]

        @block.gpsimd
        def _(g):
            g.dma_start(out=ztmp[:, :], in_=c.zg_d).then_inc(s_zl, 16)
            g.dma_start(out=stmp[:, :], in_=c.sgn_d).then_inc(s_zl, 16)

        @block.vector
        def _(v):
            v.wait_ge(s_zl, 32)
            v.tensor_tensor(out=zsg[:, :], in0=ztmp[:, :], in1=stmp[:, :], op=ALU.mult)
            v.tensor_scalar_mul(out=zsg[:, :], in0=zsg[:, :], scalar1=-1.0).then_inc(s_zs, 1)

        @block.gpsimd
        def _(g):
            g.wait_ge(s_zs, 1)
            g.dma_start(out=ztmp[:, :], in_=c.zu_d).then_inc(s_zl, 16)
            g.dma_start(out=stmp[:, :], in_=c.sun_d).then_inc(s_zl, 16)

        @block.vector
        def _(v):
            v.wait_ge(s_zl, 64)
            v.tensor_tensor(out=zsu[:, :], in0=ztmp[:, :], in1=stmp[:, :], op=ALU.mult)
            v.tensor_scalar_mul(out=zsu[:, :], in0=zsu[:, :], scalar1=-1.0).then_inc(s_zs, 1)

        @block.gpsimd
        def _(g):
            g.wait_ge(s_zs, 2)
            for jt in range(NJT):
                if jt >= 1:
                    g.wait_ge(s_dqg, 4 * jt)
                    g.wait_ge(s_dqu, 4 * jt)
                g.dma_start(out=qwg_s[:, :], in_=c.qwg_d[jt * 128:(jt + 1) * 128, :]).then_inc(s_ldq, 16)
                g.dma_start(out=qwu_s[:, :], in_=c.qwu_d[jt * 128:(jt + 1) * 128, :]).then_inc(s_ldq, 16)
                g.dma_start(out=srg_s[:, :], in_=c.srg_d[jt:jt + 1, :]).then_inc(s_ldr, 16)
                g.dma_start(out=sru_s[:, :], in_=c.sru_d[jt:jt + 1, :]).then_inc(s_ldr, 16)
            qd_s, sd_s = qwg_s, srg_s
            g.wait_ge(s_dqg, 4 * NJT)
            g.wait_ge(s_dqu, 4 * NJT)
            for jc in range(NJT):
                if jc >= 1:
                    g.wait_ge(s_ddq, 4 * jc)
                g.dma_start(out=qd_s[:, :], in_=c.qwd_d[jc * 128:(jc + 1) * 128, :]).then_inc(s_dlq, 16)
                g.dma_start(out=sd_s[:, :], in_=c.srd_d[jc:jc + 1, :]).then_inc(s_dls, 16)

        @block.tensor
        def _(t):
            t.wait_ge(s_zs, 2)
            for jt in range(NJT):
                b = jt % 2
                t.wait_ge(s_ldr, 32 * (jt + 1))
                for m, srow in ((0, srg_s), (1, sru_s)):
                    for p in range(4):
                        gp = jt * 8 + m * 4 + p
                        if gp >= 2:
                            prev = gp - 2
                            pjt, pm, pp = prev // 8, (prev % 8) // 4, prev % 4
                            t.wait_ge(s_dqg if pm == 0 else s_dqu, 4 * pjt + pp + 1)
                        ins = None
                        for nh in range(2):
                            ins = t.matmul(
                                sfb_ps[gp % 2][:, nh * 512:(nh + 1) * 512],
                                c.ones_b[:, :],
                                srow[:, p * 1024 + nh * 512: p * 1024 + (nh + 1) * 512],
                                start=True, stop=True)
                        ins.then_inc(s_bc, 1)
                for m, w1 in ((0, w1g[b]), (1, w1u[b])):
                    sdq = s_dqg if m == 0 else s_dqu
                    pps = p_gate if m == 0 else p_up
                    zsm = zsg if m == 0 else zsu
                    for hf in range(2):
                        for ic in range(NIC):
                            if hf == 0 and ic % 8 == 0:
                                t.wait_ge(sdq, 4 * jt + ic // 8 + 1)
                            t.matmul(pps[:, hf * HALF:(hf + 1) * HALF],
                                     w1[:, ic * 128:(ic + 1) * 128],
                                     c.xnb[:, ic * TOK + hf * HALF: ic * TOK + (hf + 1) * HALF],
                                     start=(ic == 0), stop=False)
                        ins = t.matmul(pps[:, hf * HALF:(hf + 1) * HALF],
                                       zsm[:, jt * 128:(jt + 1) * 128],
                                       c.xs[:, hf * HALF:(hf + 1) * HALF],
                                       start=False, stop=True)
                        if hf == 1:
                            ins.then_inc(s_mm, 1)
                if jt < NJT - 1:
                    t.wait_ge(s_silu, 2 * (jt + 1))
                    t.wait_ge(s_h, 2 * (jt + 1))
            t.wait_ge(s_h, 2 * NJT)
            for jc in range(NJT):
                t.wait_ge(s_dls, 16 * (jc + 1))
                for p in range(4):
                    gp = jc * 4 + p
                    if gp >= 2:
                        t.wait_ge(s_ddq, gp - 1)
                    ins = None
                    for nh in range(2):
                        ins = t.matmul(sfb_ps[gp % 2][:, nh * 512:(nh + 1) * 512],
                                       c.ones_b[:, :],
                                       srg_s[:, p * 1024 + nh * 512: p * 1024 + (nh + 1) * 512],
                                       start=True, stop=True)
                    ins.then_inc(s_dbc, 1)

        @block.vector
        def _(v):
            for jt in range(NJT):
                b = jt % 2
                v.wait_ge(s_ldq, 32 * (jt + 1))
                for m in range(2):
                    qws = qwg_s if m == 0 else qwu_s
                    w1 = w1g[b] if m == 0 else w1u[b]
                    sdq = s_dqg if m == 0 else s_dqu
                    for p in range(4):
                        gp = jt * 8 + m * 4 + p
                        v.wait_ge(s_bc, gp + 1)
                        if jt >= 2:
                            v.wait_ge(s_mm, 2 * (jt - 1))
                        v.tensor_tensor(out=w1[:, p * 1024:(p + 1) * 1024],
                                        in0=qws[:, p * 1024:(p + 1) * 1024],
                                        in1=sfb_ps[gp % 2][:, :],
                                        op=ALU.mult).then_inc(sdq, 1)
                hb = h_sb[jt % 2]
                for hf in range(2):
                    v.wait_ge(s_silu, 2 * jt + hf + 1)
                    v.wait_ge(s_mm, 2 * jt + 2)
                    if jt >= 2:
                        v.wait_ge(s_hstp[jt % 2], 32 * ((jt - 2) // 2 + 1))
                    v.tensor_tensor(out=hb[:, hf * HALF:(hf + 1) * HALF],
                                    in0=sg_sb[hf][:, :],
                                    in1=p_up[:, hf * HALF:(hf + 1) * HALF],
                                    op=ALU.mult).then_inc(s_h, 1)
            for jc in range(NJT):
                v.wait_ge(s_dlq, 16 * (jc + 1))
                b = jc % 2
                for p in range(4):
                    gp = jc * 4 + p
                    v.wait_ge(s_dbc, gp + 1)
                    if jc >= 2:
                        v.wait_ge(s_dstp[jc % 2], 16 * (jc // 2))
                    v.tensor_tensor(out=w1g[b][:, p * 1024:(p + 1) * 1024],
                                    in0=qwg_s[:, p * 1024:(p + 1) * 1024],
                                    in1=sfb_ps[gp % 2][:, :],
                                    op=ALU.mult).then_inc(s_ddq, 1)

        @block.scalar
        def _(s):
            for jt in range(NJT):
                for hf in range(2):
                    s.wait_ge(s_mm, 2 * jt + 1)
                    if jt >= 1:
                        s.wait_ge(s_h, 2 * jt)
                    s.activation(out=sg_sb[hf][:, :],
                                 in_=p_gate[:, hf * HALF:(hf + 1) * HALF],
                                 func=AF.Silu).then_inc(s_silu, 1)
            for jc in range(NJT):
                s.wait_ge(s_ddq, 4 * (jc + 1))
                s.dma_start(out=c.w1d_d[jc * 128:(jc + 1) * 128, :],
                            in_=w1g[jc % 2][:, :]).then_inc(s_dstp[jc % 2], 16)
            s.wait_ge(s_dstp[0], 16 * ((NJT + 1) // 2))
            s.wait_ge(s_dstp[1], 16 * (NJT // 2))

        @block.sync
        def _(sy):
            h_v = c.h_d.rearrange("(a p) t -> a p t", p=128)
            for jt in range(NJT):
                hb = h_sb[jt % 2]
                for hf in range(2):
                    sy.wait_ge(s_h, 2 * jt + hf + 1)
                    sy.dma_start(out=h_v[hf * NJT + jt, :, :],
                                 in_=hb[:, hf * HALF:(hf + 1) * HALF]).then_inc(s_hstp[jt % 2], 16)
            sy.wait_ge(s_hstp[0], 32 * ((NJT + 1) // 2))
            sy.wait_ge(s_hstp[1], 32 * (NJT // 2))
            if dbg_xnb is not None:
                sy.dma_start(out=dbg_xnb, in_=c.xnb[:, :]).then_inc(s_hstp[0], 16)
                sy.wait_ge(s_hstp[0], 32 * ((NJT + 1) // 2) + 16)

    c.boundary()

    with ExitStack() as ar:
        E = ar.enter_context
        zsd = E(nc.sbuf_tensor(tag + "_zsd", [NJT, D], BF16))
        inner = ExitStack()
        zdt = inner.enter_context(nc.sbuf_tensor(tag + "_zdt", [NJT, D], I32))
        sdt = inner.enter_context(nc.sbuf_tensor(tag + "_sdt", [NJT, D], F32))
        inner.close()
        e86_s = E(nc.sbuf_tensor(tag + "_e86", [128, NJT * NJT], BF16))
        hh = E(nc.sbuf_tensor(tag + "_hh", [128, NJT * HALF], BF16))
        hs_sb = E(nc.sbuf_tensor(tag + "_hs", [NJT, HALF], BF16))
        wd_sb = [E(nc.sbuf_tensor(f"{tag}_wd{b}", [128, 4 * 512], BF16)) for b in range(2)]
        xi_sb = [E(nc.sbuf_tensor(f"{tag}_xi{b}", [128, HALF], F32)) for b in range(2)]
        xo_sb = [E(nc.sbuf_tensor(f"{tag}_xo{b}", [128, HALF], F32)) for b in range(2)]
        p_dn = [E(nc.psum_tensor(f"{tag}_pdn{i}", [128, HALF], F32)) for i in range(4)]
        p_hs = E(nc.psum_tensor(tag + "_phs", [NJT, HALF], F32))

        s_dzs = sem(tag + "_dzs")
        s_el = c.gsem(tag + "_el")
        s_e86 = sem(tag + "_e86")
        s_hl = sem(tag + "_hl")
        s_hsm = sem(tag + "_hsm")
        s_hsv = sem(tag + "_hsv")
        s_wlp = [sem(tag + "_wl0"), sem(tag + "_wl1")]
        s_slc = sem(tag + "_slc")
        s_dmm = sem(tag + "_dmm")
        s_xlp = [c.gsem(tag + "_xl0"), c.gsem(tag + "_xl1")]
        s_ev = sem(tag + "_ev")
        s_xstp = [sem(tag + "_xst0"), sem(tag + "_xst1")]

        @block.gpsimd
        def _(g):
            g.dma_start(out=zdt[:, :], in_=c.zd_d).then_inc(s_el, 16)
            g.dma_start(out=sdt[:, :], in_=c.srd_d).then_inc(s_el, 16)

        @block.vector
        def _(v):
            v.wait_ge(s_el, 32)
            v.tensor_tensor(out=zsd[:, :], in0=zdt[:, :], in1=sdt[:, :], op=ALU.mult)
            v.tensor_scalar_mul(out=zsd[:, :], in0=zsd[:, :], scalar1=-1.0).then_inc(s_dzs, 1)

        @block.sync
        def _(sy):
            sy.wait_ge(s_dzs, 1)
            sy.dma_start(out=e86_s.ap().rearrange("p (a m) -> p a m", a=NJT),
                         in_=c.e86_d.rearrange("(a p) m -> p a m", p=128)).then_inc(s_e86, 16)

        h_v2 = c.h_d.rearrange("(hf j p) t -> hf j p t", p=128, j=NJT)
        wdv = c.w1d_d.rearrange("(j p) d -> p j d", p=128)
        JCG = [list(range(i, min(i + 4, NJT))) for i in range(0, NJT, 4)]
        NSL = len(JCG)

        for hf in range(2):
            @block.sync
            def _(sy, hf=hf):
                sy.wait_ge(s_dzs, 1)
                if hf == 1:
                    sy.wait_ge(s_dmm, 32)
                for part in range(4):
                    lo = part * 22 if part < 3 else 66
                    hi = min(lo + 22, NJT) if part < 3 else NJT
                    sy.dma_start(
                        out=hh.ap().rearrange("p (j t) -> p j t", j=NJT)[:, lo:hi, :],
                        in_=h_v2[hf, lo:hi, :, :].rearrange("j p t -> p j t"),
                    ).then_inc(s_hl, 16)

            @block.tensor
            def _(t, hf=hf):
                t.wait_ge(s_e86, 16)
                t.wait_ge(s_hl, 64 * (hf + 1))
                ins = None
                for jc in range(NJT):
                    ins = t.matmul(p_hs[0:NJT, :], e86_s[:, jc * NJT:(jc + 1) * NJT],
                                   hh[:, jc * HALF:(jc + 1) * HALF],
                                   start=(jc == 0), stop=(jc == NJT - 1))
                ins.then_inc(s_hsm, 1)

            @block.vector
            def _(v, hf=hf):
                v.wait_ge(s_hsm, hf + 1)
                v.tensor_copy(out=hs_sb[:, :], in_=p_hs[0:NJT, :]).then_inc(s_hsv, 1)

            @block.scalar
            def _(s, hf=hf):
                s.wait_ge(s_hl, 64 * (hf + 1))
                for gdc in range(8):
                    for gi, grp in enumerate(JCG):
                        seq = hf * 8 * NSL + gdc * NSL + gi
                        if seq >= 2:
                            s.wait_ge(s_slc, seq - 1)
                        s.dma_start(
                            out=wd_sb[seq % 2][:, 0:len(grp) * 512],
                            in_=wdv[:, grp[0]:grp[-1] + 1, gdc * 512:(gdc + 1) * 512],
                        ).then_inc(s_wlp[seq % 2], 16)

            @block.tensor
            def _(t, hf=hf):
                t.wait_ge(s_hsv, hf + 1)
                for gdc in range(8):
                    sgd = hf * 8 + gdc
                    if sgd >= 1:
                        t.wait_ge(s_ev, 4 * sgd)
                    for gi, grp in enumerate(JCG):
                        seq = hf * 8 * NSL + gdc * NSL + gi
                        t.wait_ge(s_wlp[seq % 2], 16 * (seq // 2 + 1))
                        ins = None
                        for ji, jc in enumerate(grp):
                            for dc in range(4):
                                ins = t.matmul(p_dn[dc][:, :],
                                               wd_sb[seq % 2][:, ji * 512 + dc * 128: ji * 512 + (dc + 1) * 128],
                                               hh[:, jc * HALF:(jc + 1) * HALF],
                                               start=(jc == 0), stop=False)
                        ins.then_inc(s_slc, 1)
                    for dc in range(4):
                        ins = t.matmul(p_dn[dc][:, :],
                                       zsd[:, (gdc * 4 + dc) * 128:(gdc * 4 + dc + 1) * 128],
                                       hs_sb[:, :], start=False, stop=True)
                        ins.then_inc(s_dmm, 1)

            @block.gpsimd
            def _(g, hf=hf):
                for gdc in range(8):
                    for dc in range(4):
                        k = hf * 32 + gdc * 4 + dc
                        if k >= 2:
                            g.wait_ge(s_ev, k - 1)
                        ch = gdc * 4 + dc
                        g.dma_start(out=xi_sb[k % 2][:, :],
                                    in_=x_in[ch * 128:(ch + 1) * 128,
                                             hf * HALF:(hf + 1) * HALF]).then_inc(s_xlp[k % 2], 16)

            @block.vector
            def _(v, hf=hf):
                for gdc in range(8):
                    for dc in range(4):
                        k = hf * 32 + gdc * 4 + dc
                        v.wait_ge(s_dmm, k + 1)
                        v.wait_ge(s_xlp[k % 2], 16 * (k // 2 + 1))
                        if k >= 2:
                            v.wait_ge(s_xstp[k % 2], 16 * (k // 2))
                        v.scalar_tensor_tensor(out=xo_sb[k % 2][:, :], in0=p_dn[dc][:, :],
                                               scalar=0.5, in1=xi_sb[k % 2][:, :],
                                               op0=ALU.mult, op1=ALU.add).then_inc(s_ev, 1)

            @block.sync
            def _(sy, hf=hf):
                for gdc in range(8):
                    for dc in range(4):
                        k = hf * 32 + gdc * 4 + dc
                        ch = gdc * 4 + dc
                        sy.wait_ge(s_ev, k + 1)
                        sy.dma_start(out=x_out[ch * 128:(ch + 1) * 128, hf * HALF:(hf + 1) * HALF],
                                     in_=xo_sb[k % 2][:, :]).then_inc(s_xstp[k % 2], 16)
                if hf == 1:
                    sy.wait_ge(s_xstp[0], 16 * 32)
                    sy.wait_ge(s_xstp[1], 16 * 32)
                    if dbg_x is not None:
                        sy.dma_start(out=dbg_x, in_=x_out).then_inc(s_xstp[0], 16)
                        sy.wait_ge(s_xstp[0], 16 * 33)

    c.boundary()


def emit_mixer(c):
    nc, block, sem = c.nc, c.block, c.sem
    MXLVL = int(os.environ.get("MK_MX", "4"))

    with ExitStack() as arP:
        EP = arP.enter_context
        k_fm = EP(nc.sbuf_tensor("mx_kfm", [128, KVH * TOK], BF16))
        v_tm = EP(nc.sbuf_tensor("mx_vtm", [128, KVH * TOK], BF16))
        S = EP(nc.sbuf_tensor("mx_S", [128, NH * 128], F32))
        dt_all = EP(nc.sbuf_tensor("mx_dt", [128, NCH * 32], F32))
        G_all = EP(nc.sbuf_tensor("mx_G", [128, NCH * 32], F32))
        cd_all = EP(nc.sbuf_tensor("mx_cd", [128, NCH * 32], F32))

        s_dtv = sem("mx_dtv")
        s_sp = sem("mx_sp")
        s_dtm = sem("mx_dtm")

        with ExitStack() as arA:
            EA = arA.enter_context
            cos_s = EA(nc.sbuf_tensor("ma_cos", [128, TOK], F32))
            sin_s = EA(nc.sbuf_tensor("ma_sin", [128, TOK], F32))
            winT_s = EA(nc.sbuf_tensor("ma_win", [128, NIC * 32], BF16))
            wsl = [EA(nc.sbuf_tensor(f"ma_w{b}", [128, NIC * 128], BF16)) for b in range(2)]
            ra = EA(nc.sbuf_tensor("ma_ra", [64, TOK], F32))
            rb2 = EA(nc.sbuf_tensor("ma_rb", [64, TOK], F32))
            q_sb = [EA(nc.sbuf_tensor(f"ma_q{b}", [128, TOK], BF16)) for b in range(2)]
            vtmp = EA(nc.sbuf_tensor("ma_vtmp", [128, TOK], BF16))
            p_pr = [EA(nc.psum_tensor(f"ma_ppr{b}", [128, TOK], F32)) for b in range(2)]
            p_dt = EA(nc.psum_tensor("ma_pdt", [128, 32], F32))
            p_vt = [EA(nc.psum_tensor(f"ma_pvt{b}", [128, 128], BF16)) for b in range(2)]

            s_cl = sem("ma_cl")
            s_cl2 = c.gsem("ma_cl2")
            s_wlp = [c.gsem("ma_wl0"), c.gsem("ma_wl1")]
            s_pm = sem("ma_pm")
            s_rp = sem("ma_rp")
            s_stp = [sem("ma_st0"), sem("ma_st1")]

            MA = int(os.environ.get("MK_MA", "9"))
            if MA >= 1:
                @block.sync
                def _(sy):
                    sy.dma_start(out=cos_s[:, :], in_=c.cos_d).then_inc(s_cl, 16)
                    sy.dma_start(out=sin_s[:, :], in_=c.sin_d).then_inc(s_cl, 16)

                @block.gpsimd
                def _(g):
                    g.dma_start(out=winT_s.ap().rearrange("p (a m) -> p a m", a=NIC),
                                in_=c.winT_d.rearrange("(a p) m -> p a m", p=128)).then_inc(s_cl2, 16)

            if MA >= 3:
                @block.gpsimd
                def _(g):
                    nsl = 0
                    for wd, noc in ((c.wq_d, NH), (c.wk_d, KVH), (c.wv_d, KVH), (c.wg_d, NH)):
                        for oc in range(noc):
                            if nsl >= 2:
                                g.wait_ge(s_pm, nsl - 1)
                            g.dma_start(out=wsl[nsl % 2][:, :],
                                        in_=wd[oc * 128:(oc + 1) * 128, :]).then_inc(s_wlp[nsl % 2], 16)
                            nsl += 1

            if MA >= 2:
                @block.tensor
                def _(t):
                    t.wait_ge(s_cl, 32)
                    t.wait_ge(s_cl2, 16)
                    for chk in range(NCH):
                        if chk >= 1:
                            t.wait_ge(s_dtv, chk)
                        ins = None
                        for ic in range(NIC):
                            ins = t.matmul(p_dt[:, :],
                                           c.xnb[:, ic * TOK + chk * 128: ic * TOK + (chk + 1) * 128],
                                           winT_s[:, ic * 32:(ic + 1) * 32],
                                           start=(ic == 0), stop=(ic == NIC - 1))
                        ins.then_inc(s_dtm, 1)

                @block.vector
                def _(v):
                    for chk in range(NCH):
                        v.wait_ge(s_dtm, chk + 1)
                        v.tensor_tensor(out=dt_all[:, chk * 32:(chk + 1) * 32], in0=p_dt[:, :],
                                        in1=c.dtb_b[:, :], op=ALU.add).then_inc(s_dtv, 1)

                @block.scalar
                def _(s):
                    for chk in range(NCH):
                        s.wait_ge(s_dtv, chk + 1)
                        gl = G_all[:, chk * 32:(chk + 1) * 32]
                        sl = dt_all[:, chk * 32:(chk + 1) * 32]
                        s.activation(out=gl, in_=sl, func=AF.Abs)
                        s.activation(out=gl, in_=gl, func=AF.Exp, scale=-1.0)
                        s.activation(out=gl, in_=gl, func=AF.Ln, bias=1.0)
                        s.activation(out=sl, in_=sl, func=AF.Relu).then_inc(s_sp, 1)

                @block.vector
                def _(v):
                    for chk in range(NCH):
                        v.wait_ge(s_sp, chk + 1)
                        sl = dt_all[:, chk * 32:(chk + 1) * 32]
                        gl = G_all[:, chk * 32:(chk + 1) * 32]
                        v.tensor_tensor(out=sl, in0=sl, in1=gl, op=ALU.add).then_inc(s_dtv, 1)

            STREAMS = [("q", NH), ("k", KVH), ("v", KVH), ("g", NH)]
            if MA < 3:
                STREAMS = []
            q_v = c.q_d.rearrange("(ch h p) t -> ch h p t", p=128, h=NH)

            s_vtc = sem("ma_vtc")
            s_vt = sem("ma_vt")
            s_vte = sem("ma_vte")

            @block.tensor
            def _(t):
                nsl = 0
                for sname, noc in STREAMS:
                    for oc in range(noc):
                        t.wait_ge(s_wlp[nsl % 2], 16 * (nsl // 2 + 1))
                        if nsl >= 2:
                            t.wait_ge(s_rp, nsl - 1)
                        pp = p_pr[nsl % 2]
                        ins = None
                        for hf in range(2):
                            for ic in range(NIC):
                                ins = t.matmul(pp[:, hf * HALF:(hf + 1) * HALF],
                                               wsl[nsl % 2][:, ic * 128:(ic + 1) * 128],
                                               c.xnb[:, ic * TOK + hf * HALF: ic * TOK + (hf + 1) * HALF],
                                               start=(ic == 0), stop=(ic == NIC - 1))
                        ins.then_inc(s_pm, 1)
                        if sname == "v":
                            t.wait_ge(s_rp, NH + KVH + oc + 1)
                            for tc in range(NCH):
                                n = oc * NCH + tc
                                if n >= 2:
                                    t.wait_ge(s_vte, n - 1)
                                t.transpose(p_vt[n % 2][:, :],
                                            vtmp[:, tc * 128:(tc + 1) * 128],
                                            c.ident_s[:, :]).then_inc(s_vt, 1)
                        nsl += 1

            @block.vector
            def _(v):
                nsl = 0
                nst = 0
                for sname, noc in STREAMS:
                    for oc in range(noc):
                        pp = p_pr[nsl % 2]
                        v.wait_ge(s_pm, nsl + 1)
                        if sname == "q" or sname == "g":
                            if nst >= 2:
                                v.wait_ge(s_stp[nst % 2], 16 * (nst // 2))
                            dst = q_sb[nst % 2]
                        if sname in ("q", "k"):
                            lo = (q_sb[nst % 2][0:64, :] if sname == "q"
                                  else k_fm[0:64, oc * TOK:(oc + 1) * TOK])
                            hi = (q_sb[nst % 2][64:128, :] if sname == "q"
                                  else k_fm[64:128, oc * TOK:(oc + 1) * TOK])
                            v.tensor_tensor(out=ra[:, :], in0=pp[0:64, :], in1=cos_s[0:64, :], op=ALU.mult)
                            v.tensor_tensor(out=rb2[:, :], in0=pp[64:128, :], in1=sin_s[0:64, :], op=ALU.mult)
                            v.tensor_tensor(out=lo, in0=ra[:, :], in1=rb2[:, :], op=ALU.subtract)
                            v.tensor_tensor(out=ra[:, :], in0=pp[64:128, :], in1=cos_s[64:128, :], op=ALU.mult)
                            v.tensor_tensor(out=rb2[:, :], in0=pp[0:64, :], in1=sin_s[64:128, :], op=ALU.mult)
                            v.tensor_tensor(out=hi, in0=ra[:, :], in1=rb2[:, :], op=ALU.add).then_inc(s_rp, 1)
                            if sname == "q":
                                nst += 1
                        elif sname == "v":
                            if oc >= 1:
                                v.wait_ge(s_vt, oc * NCH)
                            v.tensor_copy(out=vtmp[:, :], in_=pp[:, :]).then_inc(s_rp, 1)
                            for tc in range(NCH):
                                n = oc * NCH + tc
                                v.wait_ge(s_vt, n + 1)
                                v.tensor_copy(out=v_tm[:, tc * (KVH * 128) + oc * 128:
                                                       tc * (KVH * 128) + (oc + 1) * 128],
                                              in_=p_vt[n % 2][:, :]).then_inc(s_vte, 1)
                        else:
                            v.tensor_copy(out=q_sb[nst % 2][:, :], in_=pp[:, :]).then_inc(s_rp, 1)
                            nst += 1
                        nsl += 1

            @block.sync
            def _(sy):
                nsl = 0
                nst = 0
                for sname, noc in STREAMS:
                    for oc in range(noc):
                        if sname == "q":
                            sy.wait_ge(s_rp, nsl + 1)
                            sy.dma_start(out=q_v[:, oc, :, :].rearrange("ch p t -> p ch t"),
                                         in_=q_sb[nst % 2].ap().rearrange("p (ch t) -> p ch t", ch=NCH)
                                         ).then_inc(s_stp[nst % 2], 16)
                            nst += 1
                        elif sname == "g":
                            sy.wait_ge(s_rp, nsl + 1)
                            sy.dma_start(out=c.gate_d[oc * 128:(oc + 1) * 128, :],
                                         in_=q_sb[nst % 2][:, :]).then_inc(s_stp[nst % 2], 16)
                            nst += 1
                        nsl += 1
                if nst:
                    sy.wait_ge(s_stp[0], 16 * ((nst + 1) // 2))
                    sy.wait_ge(s_stp[1], 16 * (nst // 2))

        c.boundary()
        if MXLVL < 2:
            return

        s_cs = sem("mb_cs")
        s_cc = c.gsem("mb_cc")
        s_cb = sem("mb_cb")
        s_mc = sem("mb_mc")
        with ExitStack() as arB:
            EB = arB.enter_context
            g_bf = EB(nc.sbuf_tensor("mb_gbf", [128, 32], BF16))
            glast_b = EB(nc.sbuf_tensor("mb_glb", [128, NCH * 32], F32))
            ed_all = EB(nc.sbuf_tensor("mb_ed", [128, NCH * 32], F32))
            k_tm = EB(nc.sbuf_tensor("mb_ktm", [128, KVH * TOK], BF16))
            Z = EB(nc.sbuf_tensor("mb_Z", [128, NH * 128], F32))
            P_b = EB(nc.sbuf_tensor("mb_P", [128, 32], F32))
            cst_sb = EB(nc.sbuf_tensor("mb_cst", [128, NH * 128], BF16))
            xc_sb = EB(nc.sbuf_tensor("mb_xc", [128, 128], BF16))
            kd_sb = EB(nc.sbuf_tensor("mb_kd", [128, 128], BF16))
            gt_sb = EB(nc.sbuf_tensor("mb_gt", [128, 256], F32))
            Pg = EB(nc.sbuf_tensor("mc_Pg", [32, 8], F32))
            msk = EB(nc.sbuf_tensor("mc_msk", [32, 8], F32))
            Q8 = EB(nc.sbuf_tensor("mc_Q8", [32, 8], F32))
            w8 = EB(nc.sbuf_tensor("mc_w8", [32, 8], F32))
            r8 = EB(nc.sbuf_tensor("mc_r8", [32, 1], F32))
            wb = EB(nc.sbuf_tensor("mc_wb", [128, 256], F32))
            zg_sb = [EB(nc.sbuf_tensor(f"mc_zg{b}", [128, NH * 128], F32)) for b in range(2)]
            p_G = EB(nc.psum_tensor("mb_pG", [128, 32], F32))
            p_t = [EB(nc.psum_tensor(f"mb_pt{i}", [128, 128], BF16)) for i in range(2)]
            p_cs = [EB(nc.psum_tensor(f"mb_pcs{i}", [128, 128], F32)) for i in range(2)]
            p_gl = EB(nc.psum_tensor("mb_pgl", [128, 256], F32))
            p_gt = EB(nc.psum_tensor("mb_pgt", [128, 128], F32))

            s_gm = sem("mb_gm")
            s_gv = sem("mb_gv")
            s_tr = sem("mb_tr")
            s_te = sem("mb_te")
            s_gd = sem("mb_gd")
            s_zs2 = sem("mb_zs")
            s_cst = sem("mb_cst")
            s_zl2p = [c.gsem("mc_zl0"), c.gsem("mc_zl1")]

            @block.vector
            def _(v):
                v.memset(Z[:, :], 0.0).then_inc(s_gv, 1)
                for chk in range(NCH):
                    v.scalar_tensor_tensor(out=g_bf[:, :], in0=dt_all[:, chk * 32:(chk + 1) * 32],
                                           scalar=1.0, in1=c.A_b[:, :],
                                           op0=ALU.mult, op1=ALU.mult).then_inc(s_gv, 1)
                    v.wait_ge(s_gm, chk + 1)
                    v.tensor_copy(out=G_all[:, chk * 32:(chk + 1) * 32],
                                  in_=p_G[:, :]).then_inc(s_gv, 1)

            @block.tensor
            def _(t):
                for chk in range(NCH):
                    t.wait_ge(s_gv, 2 * chk + 2)
                    t.matmul(p_G[:, :], c.triu_s[:, :], g_bf[:, :],
                             start=True, stop=True).then_inc(s_gm, 1)

            @block.sync
            def _(sy):
                sy.wait_ge(s_gv, 2 * NCH + 1)
                sy.dma_start(out=c.row_d[0:1, 0:256], in_=G_all[127:128, :]).then_inc(s_gd, 16)
                sy.wait_ge(s_gd, 16)
                sy.dma_start(out=c.row_s[0:1, 0:256], in_=c.row_d[0:1, 0:256]).then_inc(s_gd, 16)

            @block.tensor
            def _(t):
                t.wait_ge(s_gd, 32)
                t.matmul(p_gl[:, :], c.ones_f[:, :], c.row_s[:, 0:256],
                         start=True, stop=True).then_inc(s_gm, 1)
                t.transpose(p_gt[:, :], G_all[:, 0:128], c.identf_s[:, :]).then_inc(s_gm, 1)

            @block.vector
            def _(v):
                v.wait_ge(s_gm, NCH + 2)
                v.tensor_copy(out=glast_b[:, :], in_=p_gl[:, :])
                v.tensor_copy(out=gt_sb[:, 0:128], in_=p_gt[:, :]).then_inc(s_gv, 2)

            @block.tensor
            def _(t):
                t.wait_ge(s_gv, 2 * NCH + 3)
                t.transpose(p_gt[:, :], G_all[:, 128:256], c.identf_s[:, :]).then_inc(s_gm, 1)

            @block.vector
            def _(v):
                v.wait_ge(s_gm, NCH + 3)
                v.tensor_copy(out=gt_sb[:, 128:256], in_=p_gt[:, :])
                v.tensor_tensor(out=ed_all[:, :], in0=glast_b[:, :], in1=G_all[:, :],
                                op=ALU.subtract).then_inc(s_gv, 2)

            @block.sync
            def _(sy):
                sy.wait_ge(s_gv, 2 * NCH + 5)
                sy.dma_start(out=c.gt_d[0:128, :], in_=gt_sb[:, 0:128]).then_inc(s_gd, 16)
                sy.dma_start(out=c.gt_d[128:256, :], in_=gt_sb[:, 128:256]).then_inc(s_gd, 16)
                sy.wait_ge(s_gd, 64)

            @block.scalar
            def _(s):
                s.wait_ge(s_gv, 2 * NCH + 5)
                s.activation(out=ed_all[:, :], in_=ed_all[:, :], func=AF.Exp)
                s.activation(out=cd_all[:, :], in_=glast_b[:, :], func=AF.Exp).then_inc(s_gv, 1)

            @block.vector
            def _(v):
                v.wait_ge(s_gv, 2 * NCH + 6)
                v.tensor_copy(out=P_b[:, :], in_=glast_b[:, 0:32])
                for chk in range(1, NCH):
                    v.tensor_tensor(out=P_b[:, :], in0=P_b[:, :],
                                    in1=glast_b[:, chk * 32:(chk + 1) * 32],
                                    op=ALU.add)
                v.tensor_scalar_add(out=P_b[:, :], in0=P_b[:, :], scalar1=0.0).then_inc(s_gv, 1)

            @block.scalar
            def _(s):
                s.wait_ge(s_gv, 2 * NCH + 7)
                s.activation(out=P_b[:, :], in_=P_b[:, :], func=AF.Exp).then_inc(s_gv, 1)

            GPRE = 2 * NCH + 8

            @block.tensor
            def _(t):
                n = 0
                for kv in range(KVH):
                    for chk in range(NCH):
                        if n >= 2:
                            t.wait_ge(s_te, n - 1)
                        t.transpose(p_t[n % 2][:, :],
                                    k_fm[:, kv * TOK + chk * 128: kv * TOK + (chk + 1) * 128],
                                    c.ident_s[:, :]).then_inc(s_tr, 1)
                        n += 1

            @block.vector
            def _(v):
                n = 0
                for kv in range(KVH):
                    for chk in range(NCH):
                        v.wait_ge(s_tr, n + 1)
                        v.tensor_copy(out=k_tm[:, chk * (KVH * 128) + kv * 128:
                                              chk * (KVH * 128) + (kv + 1) * 128],
                                      in_=p_t[n % 2][:, :]).then_inc(s_te, 1)
                        n += 1

            @block.vector
            def _(v):
                v.wait_ge(s_gv, GPRE)
                v.wait_ge(s_te, 64)
                for chk in range(NCH):
                    for h in range(NH):
                        kv = h // 4
                        i = chk * NH + h
                        col = chk * 32 + h
                        base = chk * (KVH * 128) + kv * 128
                        v.tensor_scalar_mul(out=xc_sb[:, :], in0=v_tm[:, base:base + 128],
                                            scalar1=dt_all[:, col:col + 1])
                        v.tensor_scalar_mul(out=kd_sb[:, :], in0=k_tm[:, base:base + 128],
                                            scalar1=ed_all[:, col:col + 1]).then_inc(s_gv, 1)
                        v.wait_ge(s_cs, i + 1)
                        v.scalar_tensor_tensor(out=Z[:, h * 128:(h + 1) * 128],
                                               in0=Z[:, h * 128:(h + 1) * 128],
                                               scalar=cd_all[:, col:col + 1],
                                               in1=p_cs[i % 2][:, :],
                                               op0=ALU.mult, op1=ALU.add)
                        v.tensor_copy(out=cst_sb[:, h * 128:(h + 1) * 128],
                                      in_=p_cs[i % 2][:, :]).then_inc(s_zs2, 1)

            @block.tensor
            def _(t):
                for chk in range(NCH):
                    for h in range(NH):
                        i = chk * NH + h
                        t.wait_ge(s_gv, GPRE + i + 1)
                        if i >= 2:
                            t.wait_ge(s_zs2, i - 1)
                        t.matmul(p_cs[i % 2][:, :], kd_sb[:, :], xc_sb[:, :],
                                 start=True, stop=True).then_inc(s_cs, 1)

            @block.scalar
            def _(s):
                for chk in range(NCH):
                    s.wait_ge(s_zs2, (chk + 1) * NH)
                    s.dma_start(out=c.cst_d[chk * 128:(chk + 1) * 128, :],
                                in_=cst_sb[:, :]).then_inc(s_cst, 16)

            @block.vector
            def _(v):
                v.wait_ge(s_gv, GPRE)
                v.memset(c.row_s[:, 256:384], 0.0)
                v.tensor_copy(out=c.row_s[:, 256:288], in_=P_b[0:1, :]).then_inc(s_gv, 1)

            @block.sync
            def _(sy):
                sy.wait_ge(s_zs2, NCH * NH)
                sy.dma_start(out=c.ccin_d.ap()[0:NH * 128, :].rearrange("(h d) e -> d h e", d=128),
                             in_=Z.ap().rearrange("d (h e) -> d h e", h=NH)).then_inc(s_cb, 16)
                sy.dma_start(out=c.ccin_d.ap()[NH * 128:NH * 128 + 1, :],
                             in_=c.row_s[0:1, 256:384]).then_inc(s_cb, 16)

            @block.gpsimd
            def _(g):
                g.wait_ge(s_cb, 32)
                g.wait_ge(s_cst, 16 * NCH)
                g.collective_compute(
                    "AllGather", ALU.bypass, replica_groups=[list(range(NCORE))],
                    ins=[c.ccin_d.ap().opt()], outs=[c.ccout_d.ap().opt()],
                ).then_inc(s_cc, 1)

            ROWS = NH * 128 + 1
            ccv = c.ccout_d.ap().rearrange("(j r) e -> j r e", r=ROWS)

            @block.sync
            def _(sy):
                sy.wait_ge(s_cc, 1)
                sy.dma_start(out=msk[:, :], in_=c.cmask_d).then_inc(s_mc, 16)
                sy.dma_start(out=Pg[:, :],
                             in_=ccv[:, ROWS - 1:ROWS, 0:32].rearrange("j o e -> e (o j)"),
                             allow_slow_non_contiguous=True).then_inc(s_mc, 16)

            @block.vector
            def _(v):
                v.wait_ge(s_mc, 32)
                v.tensor_scalar_add(out=Q8[:, :], in0=Pg[:, :], scalar1=-1.0)
                v.tensor_tensor(out=Q8[:, :], in0=Q8[:, :], in1=msk[:, :], op=ALU.mult)
                v.tensor_scalar_add(out=Q8[:, :], in0=Q8[:, :], scalar1=1.0)
                v.memset(r8[:, :], 1.0)
                for j in range(7, -1, -1):
                    v.tensor_tensor(out=w8[:, j:j + 1], in0=msk[:, j:j + 1], in1=r8[:, :], op=ALU.mult)
                    v.tensor_tensor(out=r8[:, :], in0=r8[:, :], in1=Q8[:, j:j + 1], op=ALU.mult)
                v.tensor_scalar_add(out=w8[:, :], in0=w8[:, :], scalar1=0.0).then_inc(s_mc, 1)

            @block.sync
            def _(sy):
                sy.wait_ge(s_mc, 33)
                sy.dma_start(out=c.w8_d.rearrange("o (h j) -> h (o j)", h=32),
                             in_=w8[:, :]).then_inc(s_mc, 16)
                sy.wait_ge(s_mc, 49)
                sy.dma_start(out=c.row_s[0:1, 0:256], in_=c.w8_d).then_inc(s_mc, 16)

            @block.tensor
            def _(t):
                t.wait_ge(s_mc, 65)
                t.matmul(p_gl[:, :], c.ones_f[:, :], c.row_s[:, 0:256],
                         start=True, stop=True).then_inc(s_mc, 1)

            @block.vector
            def _(v):
                v.wait_ge(s_mc, 66)
                v.tensor_copy(out=wb[:, :], in_=p_gl[:, :])
                v.memset(S[:, :], 0.0).then_inc(s_mc, 1)

            @block.gpsimd
            def _(g):
                for j in range(NCORE):
                    if j >= 2:
                        g.wait_ge(s_mc, 67 + (j - 1) * NH)
                    g.dma_start(out=zg_sb[j % 2][:, :].rearrange("d (h e) -> d h e", h=NH),
                                in_=ccv[j, 0:NH * 128, :].rearrange("(h d) e -> d h e", d=128)
                                ).then_inc(s_zl2p[j % 2], 16)

            @block.vector
            def _(v):
                v.wait_ge(s_mc, 67)
                for j in range(NCORE):
                    v.wait_ge(s_zl2p[j % 2], 16 * (j // 2 + 1))
                    for h in range(NH):
                        v.scalar_tensor_tensor(out=S[:, h * 128:(h + 1) * 128],
                                               in0=zg_sb[j % 2][:, h * 128:(h + 1) * 128],
                                               scalar=wb[:, h * 8 + j:h * 8 + j + 1],
                                               in1=S[:, h * 128:(h + 1) * 128],
                                               op0=ALU.mult, op1=ALU.add).then_inc(s_mc, 1)

            if DBG:
                @block.sync
                def _(sy):
                    sy.wait_ge(s_mc, 67 + NCORE * NH)
                    sy.dma_start(out=c.dbg["S"], in_=S[:, :]).then_inc(s_mc, 16)
                    sy.dma_start(out=c.dbg["gall"], in_=G_all[:, :]).then_inc(s_mc, 16)
                    sy.dma_start(out=c.dbg["dtall"], in_=dt_all[:, :]).then_inc(s_mc, 16)
                    sy.dma_start(out=c.dbg["kfm"], in_=k_fm[:, :]).then_inc(s_mc, 16)
                    sy.dma_start(out=c.dbg["cdall"], in_=cd_all[:, :]).then_inc(s_mc, 16)
                    sy.dma_start(out=c.dbg["Z"], in_=Z[:, :]).then_inc(s_mc, 16)
                    sy.dma_start(out=c.dbg["wb"], in_=wb[:, :]).then_inc(s_mc, 16)
                    sy.dma_start(out=c.dbg["cst"], in_=c.cst_d).then_inc(s_mc, 16)
                    sy.wait_ge(s_mc, 67 + NCORE * NH + 128)

        c.boundary()
        if MXLVL < 3:
            return

        yg_fm = c.xnb
        with ExitStack() as arD:
            ED = arD.enter_context
            q_ch = [ED(nc.sbuf_tensor(f"md_q{b}", [128, NH * 128], BF16)) for b in range(2)]
            cs_ch = [ED(nc.sbuf_tensor(f"md_cs{b}", [128, NH * 128], BF16)) for b in range(2)]
            gt_ch = [ED(nc.sbuf_tensor(f"md_gt{b}", [128, NH * 128], BF16)) for b in range(2)]
            grow = ED(nc.sbuf_tensor("md_gr", [1, NH * 128], F32))
            sgate = ED(nc.sbuf_tensor("md_sg", [128, NH * 128], BF16))
            arg_s = ED(nc.sbuf_tensor("md_arg", [128, 128], F32))
            dexp_s = ED(nc.sbuf_tensor("md_dexp", [128, 128], F32))
            eg_s = ED(nc.sbuf_tensor("md_eg", [128, 128], F32))
            mt_s = ED(nc.sbuf_tensor("md_mt", [128, 128], BF16))
            qg_s = ED(nc.sbuf_tensor("md_qg", [128, 128], BF16))
            xc2 = ED(nc.sbuf_tensor("md_xc", [128, 128], BF16))
            pv_s = ED(nc.sbuf_tensor("md_pv", [128, 128], BF16))
            y_s = ED(nc.sbuf_tensor("md_y", [128, 128], BF16))
            jnk = ED(nc.sbuf_tensor("md_jnk", [128, 128], F32))
            ms_c = ED(nc.sbuf_tensor("md_ms", [128, 1], F32))
            rst_c = ED(nc.sbuf_tensor("md_rst", [128, 1], F32))
            p_gib = [ED(nc.psum_tensor(f"md_pgib{i}", [128, 128], F32)) for i in range(2)]
            p_sc = [ED(nc.psum_tensor(f"md_psc{i}", [128, 128], F32)) for i in range(2)]
            p_y = [ED(nc.psum_tensor(f"md_py{i}", [128, 128], F32)) for i in range(2)]
            p_tr2 = [ED(nc.psum_tensor(f"md_ptr{i}", [128, 128], BF16)) for i in range(2)]

            s_pf = sem("md_pf")
            s_gb = sem("md_gb")
            s_sc2 = sem("md_sc")
            s_dv = sem("md_dv")
            s_ac = sem("md_ac")
            s_ym = sem("md_ym")
            s_tm2 = sem("md_tm")
            s_yg = sem("md_yg")

            q_v2 = c.q_d.rearrange("(ch a) t -> ch a t", a=NH * 128)
            gt_rows = c.gt_d.rearrange("(ch a) b -> ch (a b)", a=32)

            @block.sync
            def _(sy):
                for chk in range(NCH):
                    if chk >= 2:
                        sy.wait_ge(s_yg, (chk - 1) * NH)
                    sy.dma_start(out=q_ch[chk % 2][:, :].rearrange("p (h t) -> p h t", h=NH),
                                 in_=q_v2[chk].rearrange("(h p) t -> p h t", p=128)).then_inc(s_pf, 16)
                    sy.dma_start(out=cs_ch[chk % 2][:, :],
                                 in_=c.cst_d[chk * 128:(chk + 1) * 128, :]).then_inc(s_pf, 16)
                    sy.dma_start(out=gt_ch[chk % 2][:, :].rearrange("p (h t) -> p h t", h=NH),
                                 in_=c.gate_d.rearrange("(h p) t -> p h t", p=128)[:, :, chk * 128:(chk + 1) * 128]
                                 ).then_inc(s_pf, 16)
                    if chk >= 1:
                        sy.wait_ge(s_gb, chk * NH)
                    sy.dma_start(out=grow[:, :],
                                 in_=gt_rows[chk:chk + 1, :]).then_inc(s_pf, 16)

            def ac_count(chk, h, step):
                full = chk * (1 + 2 * NH)
                return full + 1 + 2 * h + step

            @block.scalar
            def _(s):
                for chk in range(NCH):
                    s.wait_ge(s_pf, 64 * (chk + 1))
                    s.activation(out=sgate[:, :], in_=gt_ch[chk % 2][:, :],
                                 func=AF.Silu).then_inc(s_ac, 1)
                    for h in range(NH):
                        i = chk * NH + h
                        s.wait_ge(s_dv, i * 8 + 1)
                        s.activation(out=dexp_s[:, :], in_=arg_s[:, :], func=AF.Exp)
                        s.activation(out=eg_s[:, :], in_=p_gib[i % 2][:, :],
                                     func=AF.Exp).then_inc(s_ac, 1)
                        s.wait_ge(s_ym, i + 1)
                        s.activation(out=jnk[:, :], in_=p_y[i % 2][:, :], func=AF.Square,
                                     accum_out=ms_c[:, :])
                        s.activation(out=rst_c[:, :], in_=ms_c[:, :], func=AF.Ln,
                                     scale=1.0 / 128.0, bias=RMS_EPS)
                        s.activation(out=rst_c[:, :], in_=rst_c[:, :], func=AF.Exp,
                                     scale=-0.5).then_inc(s_ac, 1)

            @block.tensor
            def _(t):
                for chk in range(NCH):
                    t.wait_ge(s_pf, 64 * (chk + 1))
                    for h in range(NH):
                        i = chk * NH + h
                        kv = h // 4
                        if i >= 2:
                            pchk, ph = (i - 2) // NH, (i - 2) % NH
                            t.wait_ge(s_ac, ac_count(pchk, ph, 1))
                        t.matmul(p_gib[i % 2][:, :], c.ones_f[:, :],
                                 grow[:, h * 128:(h + 1) * 128],
                                 start=True, stop=True).then_inc(s_gb, 1)
                        if i >= 2:
                            t.wait_ge(s_dv, (i - 2) * 8 + 2)
                        t.matmul(p_sc[i % 2][:, :],
                                 k_fm[:, kv * TOK + chk * 128: kv * TOK + (chk + 1) * 128],
                                 q_ch[chk % 2][:, h * 128:(h + 1) * 128],
                                 start=True, stop=True).then_inc(s_sc2, 1)
                        t.wait_ge(s_dv, i * 8 + 6)
                        t.matmul(p_y[i % 2][:, :], mt_s[:, :], xc2[:, :], start=True, stop=False)
                        t.matmul(p_y[i % 2][:, :], qg_s[:, :], pv_s[:, :],
                                 start=False, stop=True).then_inc(s_ym, 1)
                        t.wait_ge(s_dv, i * 8 + 7)
                        t.transpose(p_tr2[i % 2][:, :], y_s[:, :], c.ident_s[:, :]).then_inc(s_tm2, 1)

            @block.vector
            def _(v):
                for chk in range(NCH):
                    for h in range(NH):
                        i = chk * NH + h
                        kv = h // 4
                        col = chk * 32 + h
                        base = chk * (KVH * 128) + kv * 128
                        v.wait_ge(s_gb, i + 1)
                        v.scalar_tensor_tensor(out=arg_s[:, :], in0=p_gib[i % 2][:, :],
                                               scalar=G_all[:, col:col + 1], in1=c.mask_s[:, :],
                                               op0=ALU.subtract, op1=ALU.add).then_inc(s_dv, 1)
                        v.wait_ge(s_ac, ac_count(chk, h, 1))
                        v.wait_ge(s_sc2, i + 1)
                        v.tensor_tensor(out=mt_s[:, :], in0=p_sc[i % 2][:, :], in1=dexp_s[:, :],
                                        op=ALU.mult).then_inc(s_dv, 1)
                        v.tensor_tensor(out=qg_s[:, :], in0=q_ch[chk % 2][:, h * 128:(h + 1) * 128],
                                        in1=eg_s[:, :], op=ALU.mult).then_inc(s_dv, 1)
                        v.tensor_scalar_mul(out=xc2[:, :], in0=v_tm[:, base:base + 128],
                                            scalar1=dt_all[:, col:col + 1]).then_inc(s_dv, 1)
                        v.tensor_copy(out=pv_s[:, :], in_=S[:, h * 128:(h + 1) * 128]).then_inc(s_dv, 1)
                        v.scalar_tensor_tensor(out=S[:, h * 128:(h + 1) * 128],
                                               in0=S[:, h * 128:(h + 1) * 128],
                                               scalar=cd_all[:, col:col + 1],
                                               in1=cs_ch[chk % 2][:, h * 128:(h + 1) * 128],
                                               op0=ALU.mult, op1=ALU.add).then_inc(s_dv, 1)
                        v.wait_ge(s_ac, ac_count(chk, h, 2))
                        v.tensor_scalar_mul(out=y_s[:, :], in0=p_y[i % 2][:, :],
                                            scalar1=rst_c[:, :]).then_inc(s_dv, 1)
                        v.wait_ge(s_tm2, i + 1)
                        v.scalar_tensor_tensor(out=yg_fm[:, h * TOK + chk * 128: h * TOK + (chk + 1) * 128],
                                               in0=p_tr2[i % 2][:, :], scalar=c.gnorm_s[:, :],
                                               in1=sgate[:, h * 128:(h + 1) * 128],
                                               op0=ALU.mult, op1=ALU.mult).then_inc(s_dv, 1)
                    v.wait_ge(s_dv, (chk + 1) * NH * 8)
                    v.tensor_scalar_add(out=jnk[:, 0:1], in0=jnk[:, 0:1],
                                        scalar1=0.0).then_inc(s_yg, NH)

            if DBG:
                @block.scalar
                def _(s):
                    s.wait_ge(s_yg, NCH * NH)
                    s.dma_start(out=c.dbg["ygfm"], in_=yg_fm[:, :]).then_inc(s_ac, 16)
                    s.dma_start(out=c.dbg["Sfin"], in_=S[:, :]).then_inc(s_ac, 16)
                    s.wait_ge(s_ac, NCH * (1 + 2 * NH) + 32)

        c.boundary()
        if MXLVL < 4:
            return

        with ExitStack() as arE:
            EE = arE.enter_context
            wsl2 = [EE(nc.sbuf_tensor(f"me_w{b}", [128, NIC * 128], BF16)) for b in range(2)]
            xi2 = [EE(nc.sbuf_tensor(f"me_xi{b}", [128, TOK], F32)) for b in range(2)]
            xo2 = [EE(nc.sbuf_tensor(f"me_xo{b}", [128, TOK], F32)) for b in range(2)]
            p_o = [EE(nc.psum_tensor(f"me_po{i}", [128, HALF], F32)) for i in range(4)]

            s_wl2p = [c.gsem("me_wl0"), c.gsem("me_wl1")]
            s_om = sem("me_om")
            s_xl2p = [sem("me_xl0"), sem("me_xl1")]
            s_ev2 = sem("me_ev")
            s_xs2p = [sem("me_xst0"), sem("me_xst1")]

            @block.gpsimd
            def _(g):
                for oc in range(NH):
                    if oc >= 2:
                        g.wait_ge(s_om, 2 * (oc - 1))
                    g.dma_start(out=wsl2[oc % 2][:, :],
                                in_=c.wo_d[oc * 128:(oc + 1) * 128, :]).then_inc(s_wl2p[oc % 2], 16)

            @block.sync
            def _(sy):
                for oc in range(NH):
                    if oc >= 2:
                        sy.wait_ge(s_ev2, 2 * (oc - 1))
                    sy.dma_start(out=xi2[oc % 2][:, :],
                                 in_=c.x1_d[oc * 128:(oc + 1) * 128, :]).then_inc(s_xl2p[oc % 2], 16)

            @block.tensor
            def _(t):
                for oc in range(NH):
                    t.wait_ge(s_wl2p[oc % 2], 16 * (oc // 2 + 1))
                    if oc >= 2:
                        t.wait_ge(s_ev2, 2 * (oc - 1))
                    for hf in range(2):
                        pb = p_o[(oc % 2) * 2 + hf]
                        ins = None
                        for hc in range(NH):
                            ins = t.matmul(pb[:, :],
                                           wsl2[oc % 2][:, hc * 128:(hc + 1) * 128],
                                           yg_fm[:, hc * TOK + hf * HALF: hc * TOK + (hf + 1) * HALF],
                                           start=(hc == 0), stop=(hc == NH - 1))
                        ins.then_inc(s_om, 1)

            @block.vector
            def _(v):
                for oc in range(NH):
                    v.wait_ge(s_xl2p[oc % 2], 16 * (oc // 2 + 1))
                    if oc >= 2:
                        v.wait_ge(s_xs2p[oc % 2], 16 * (oc // 2))
                    for hf in range(2):
                        v.wait_ge(s_om, 2 * oc + hf + 1)
                        v.tensor_tensor(out=xo2[oc % 2][:, hf * HALF:(hf + 1) * HALF],
                                        in0=p_o[(oc % 2) * 2 + hf][:, :],
                                        in1=xi2[oc % 2][:, hf * HALF:(hf + 1) * HALF],
                                        op=ALU.add).then_inc(s_ev2, 1)

            @block.scalar
            def _(s):
                for oc in range(NH):
                    s.wait_ge(s_ev2, 2 * (oc + 1))
                    s.dma_start(out=c.x2_d[oc * 128:(oc + 1) * 128, :],
                                in_=xo2[oc % 2][:, :]).then_inc(s_xs2p[oc % 2], 16)
                s.wait_ge(s_xs2p[0], 16 * 16)
                s.wait_ge(s_xs2p[1], 16 * 16)
                if DBG:
                    s.dma_start(out=c.dbg["x2"], in_=c.x2_d).then_inc(s_xs2p[0], 16)
                    s.wait_ge(s_xs2p[0], 16 * 17)

    c.boundary()


_CACHE = {}


def kernel(**inputs):
    in_maps = make_in_maps(inputs)
    if "nc" not in _CACHE:
        _CACHE["nc"], _CACHE["c"] = build_program()
    nc = _CACHE["nc"]
    res = run_bass_kernel_spmd(nc, in_maps, core_ids=list(range(NCORE)))
    out = np.empty((2, 4096, 4096), dtype=np.float32)
    for ci in range(NCORE):
        b, t0 = ci // 4, (ci % 4) * TOK
        out[b, t0:t0 + TOK, :] = res.results[ci]["out_fm"].T
    if DBG:
        kernel._last = res
    return out
